# version 06 (final)
# baseline (speedup 1.0000x reference)
import contextlib
import numpy as np
import ml_dtypes

import concourse.bass as bass
import concourse.mybir as mybir
from concourse.alu_op_type import AluOpType as OP
from concourse.bass_utils import run_bass_kernel_spmd

F32 = mybir.dt.float32
BF16 = mybir.dt.bfloat16
I32 = mybir.dt.int32
I8 = mybir.dt.int8
AF = mybir.ActivationFunctionType
AX = mybir.AxisListType

NCORE = 8
B_, L_, HID = 2, 4096, 4096
H, KVH, D, CH = 32, 8, 128, 128
T = 1024
NC = T // CH
KC = HID // 128
I_FF = 11008
GFF = I_FF // 128
NB = I_FF // 256
OBN = HID // 256
RMS_EPS = 1e-5
LN_EPS = 1e-5
ROPE_THETA = 10000.0

STOP_AFTER = 99
DBG = 0
import os as _os
P2SLABS = int(_os.environ.get("P2SLABS", "17"))
SKIP_CC = int(_os.environ.get("SKIP_CC", "0"))
NO_ZS = int(_os.environ.get("NO_ZS", "0"))
NO_SB = int(_os.environ.get("NO_SB", "0"))
P2PARTS = int(_os.environ.get("P2PARTS", "7"))


def _eo_order(n):
    r = np.arange(64)
    lo = 8 * (r // 4) + 2 * (r % 4)
    pat = np.concatenate([lo, lo + 1])
    return (np.arange(n // 128)[:, None] * 128 + pat[None, :]).ravel()


def _tau_order():
    j = np.arange(KC)[:, None]
    p = np.arange(128)[None, :]
    return ((p // 4) * 128 + 4 * j + (p % 4)).ravel()


def _rope_tables(pos):
    inv = 1.0 / ROPE_THETA ** (np.arange(0, D, 2, dtype=np.float64) / D)
    ang = pos[:, None].astype(np.float64) * inv[None, :]
    cos = np.concatenate([np.cos(ang), np.cos(ang)], axis=-1)
    sin = np.concatenate([np.sin(ang), np.sin(ang)], axis=-1)
    return cos.astype(np.float32), sin.astype(np.float32)


def _retile(w, ncols_pad=None):
    hid, n = w.shape
    if ncols_pad and ncols_pad > n:
        w = np.concatenate([w, np.zeros((hid, ncols_pad - n), w.dtype)], axis=1)
        n = ncols_pad
    return np.ascontiguousarray(
        w.reshape(KC, 128, n // 128, 128).transpose(2, 1, 0, 3))


def _host_prep(inp):
    eo = _eo_order(HID)
    eoI = _eo_order(I_FF)
    tau = _tau_order()
    eotau = eo[tau]
    bf = ml_dtypes.bfloat16

    x = np.asarray(inp["L0_attn_hidden_states"], dtype=np.float32)
    wkv_flat = np.concatenate(
        [np.asarray(inp["L0_attn_Wk"], dtype=np.float32).T,
         np.asarray(inp["L0_attn_Wv"], dtype=np.float32).T,
         np.asarray(inp["L0_attn_Win"], dtype=np.float32).T], axis=1)[eo, :]
    wkv_t = _retile(wkv_flat, 17 * 128)
    wq_t = _retile(np.asarray(inp["L0_attn_Wq"], dtype=np.float32).T[eo, :])
    wg_t = _retile(np.asarray(inp["L0_attn_Wg"], dtype=np.float32).T[eo, :])
    wo_t = _retile(np.asarray(inp["L0_attn_Wo"], dtype=np.float32).T[:, eo])

    qwg = np.asarray(inp["L0_ffn_gate_qweight"], dtype=np.int32)
    qwu = np.asarray(inp["L0_ffn_up_qweight"], dtype=np.int32)
    qwd = np.asarray(inp["L0_ffn_down_qweight"], dtype=np.int32)

    def _tile_gu(q):
        return np.ascontiguousarray(
            q[eotau, :].reshape(KC, 128, NB, 32).transpose(2, 1, 0, 3))
    qwg_t = _tile_gu(qwg)
    qwu_t = _tile_gu(qwu)
    qwd_t = np.ascontiguousarray(
        qwd[eoI, :].reshape(GFF, 128, OBN, 32).transpose(2, 1, 0, 3))

    sg = np.asarray(inp["L0_ffn_gate_scales"], dtype=np.float32)[:, eoI]
    su = np.asarray(inp["L0_ffn_up_scales"], dtype=np.float32)[:, eoI]
    sd = np.asarray(inp["L0_ffn_down_scales"], dtype=np.float32)[:, eo]
    sgt = np.ascontiguousarray(np.repeat(sg, 4, axis=0))
    sut = np.ascontiguousarray(np.repeat(su, 4, axis=0))
    sdc = np.ascontiguousarray(sd)
    zg = np.ascontiguousarray(np.asarray(inp["L0_ffn_gate_qzeros"], dtype=np.int32))
    zu = np.ascontiguousarray(np.asarray(inp["L0_ffn_up_qzeros"], dtype=np.int32))
    zd = np.ascontiguousarray(np.asarray(inp["L0_ffn_down_qzeros"], dtype=np.int32))

    dtb = np.ascontiguousarray(np.asarray(inp["L0_attn_dt_bias"], dtype=np.float32)[None, :])
    alog = np.ascontiguousarray(np.asarray(inp["L0_attn_A_log_bias"], dtype=np.float32)[None, :])
    gncol = np.ascontiguousarray(np.asarray(inp["L0_attn_gnorm_w"], dtype=np.float32)[:, None])

    jj = np.arange(CH)
    consts = dict(
        trilm=(jj[:, None] <= jj[None, :]).astype(np.float32),
        triu=(jj[:, None] > jj[None, :]).astype(np.float32),
        maskT=(jj[:, None] <= jj[None, :]).astype(np.float32).astype(bf),
        identf=np.eye(128, dtype=np.float32),
        identb=np.eye(128, dtype=np.float32).astype(bf),
        ones1f=np.ones((1, 128), dtype=np.float32),
        ones1b=np.ones((1, 128), dtype=np.float32).astype(bf),
        onescf=np.ones((128, 1), dtype=np.float32),
        onescb=np.ones((128, 1), dtype=np.float32).astype(bf),
        mgsel=(np.arange(128)[:, None] // 4 ==
               np.arange(32)[None, :]).astype(np.float32).astype(bf),
    )
    oh86 = np.zeros((128, 171), dtype=np.float32)
    oh86[:, 85] = 1.0
    consts["oh86"] = oh86.astype(bf)

    cv = np.zeros((128, 8), dtype=np.float32)
    for i, val in enumerate([-1.0, -0.5, 1.0 / 128, 1e-5, 1.0 / HID]):
        cv[:, i] = val
    consts["cvals"] = cv
    shared = dict(consts)
    shared.update(
        wkv_t=wkv_t, wq_t=wq_t, wg_t=wg_t, wo_t=wo_t,
        dtb=dtb, alog=alog, gncol=gncol,
        qwg_t=qwg_t, qwu_t=qwu_t, qwd_t=qwd_t,
        sgt=sgt, sut=sut, sdc=sdc, zg=zg, zu=zu, zd=zd,
    )
    maps = []
    for r in range(NCORE):
        b, q = r // 4, r % 4
        p0 = q * T
        cosf, sinf = _rope_tables(np.arange(p0, p0 + T))
        cmask = np.zeros((1, NCORE), dtype=np.float32)
        for m in range(NCORE):
            if m // 4 == b and m % 4 < q:
                cmask[0, m] = 1.0
        mm = dict(shared)
        mm.update(xT=np.ascontiguousarray(x[b, p0:p0 + T, :].T[eo, :]),
                  cosf=np.ascontiguousarray(cosf.T.astype(bf)),
                  sinf=np.ascontiguousarray(sinf.T.astype(bf)),
                  cmask=cmask)
        maps.append(mm)
    return maps, eo


class _EngProxy:
    def __init__(self, eng, sem):
        self._e = eng
        self._s = sem
        self.count = 0

    def dma_start(self, *a, **k):
        ins = self._e.dma_start(*a, **k)
        ins.then_inc(self._s, 16)
        self.count += 1
        return ins

    def __getattr__(self, n):
        return getattr(self._e, n)


class Sched:
    def __init__(self, nc, st):
        self.nc = nc
        self.st = st
        self.sems = {}
        self.cnt = {}
        self.steps = []

    def sem(self, name):
        if name not in self.sems:
            self.sems[name] = self.st.enter_context(self.nc.semaphore(name))
            self.cnt[name] = 0
        return self.sems[name]

    DMAQ = ("dq", "dg", "ds")

    def do(self, engine, fn, waits=(), inc=None):
        wl = [(self.sem(s), v) for s, v in waits if v > 0]
        iname = n = None
        if inc is not None:
            iname, n = inc
            self.sem(iname)
            self.cnt[iname] += n

        def run(eng, wl=wl, fn=fn, iname=iname, n=n):
            for s, v in wl:
                eng.wait_ge(s, v)
            if iname is not None and n % 16 == 0:
                px = _EngProxy(eng, self.sems[iname])
                fn(px)
                assert px.count * 16 == n, (
                    f"dma count mismatch: {px.count} vs {n // 16}")
            else:
                last = fn(eng)
                if iname is not None:
                    assert last is not None
                    last.then_inc(self.sems[iname], n)
        self.steps.append((engine, run))
        return {k: self.cnt[k] for k in self.cnt}

    def now(self, name):
        self.sem(name)
        return self.cnt[name]

    def emit(self, block):
        for e in ["sync", "gpsimd", "scalar", "vector", "tensor"]:
            mine = [fn for en, fn in self.steps if en == e]
            if not mine:
                continue

            def body(eng, mine=mine):
                for fn in mine:
                    fn(eng)
            getattr(block, e)(body)


def build_nc(stop_after=99, dbg=0):
    nc = bass.Bass("TRN2", target_bir_lowering=False)

    def din(name, shape, dt):
        return nc.dram_tensor(name, shape, dt, kind="ExternalInput").ap()

    xT = din("xT", [HID, T], F32)
    wkv_d = din("wkv_t", [17, 128, KC, 128], F32)
    wq_d = din("wq_t", [KC, 128, KC, 128], F32)
    wg_d = din("wg_t", [KC, 128, KC, 128], F32)
    wo_d = din("wo_t", [KC, 128, KC, 128], F32)
    cosf_d = din("cosf", [128, T], BF16)
    sinf_d = din("sinf", [128, T], BF16)
    dtb_d = din("dtb", [1, H], F32)
    alog_d = din("alog", [1, H], F32)
    gncol_d = din("gncol", [128, 1], F32)
    cmask_d = din("cmask", [1, NCORE], F32)
    trilm_d = din("trilm", [CH, CH], F32)
    triu_d = din("triu", [CH, CH], F32)
    identf_d = din("identf", [128, 128], F32)
    identb_d = din("identb", [128, 128], BF16)
    maskT_d = din("maskT", [CH, CH], BF16)
    ones1f_d = din("ones1f", [1, 128], F32)
    cvals_d = din("cvals", [128, 8], F32)
    ones1b_d = din("ones1b", [1, 128], BF16)
    onescf_d = din("onescf", [128, 1], F32)
    onescb_d = din("onescb", [128, 1], BF16)
    mgsel_d = din("mgsel", [128, 32], BF16)
    oh86_d = din("oh86", [128, 171], BF16)
    qwg_d = din("qwg_t", [NB, 128, KC, 32], I32)
    qwu_d = din("qwu_t", [NB, 128, KC, 32], I32)
    qwd_d = din("qwd_t", [OBN, 128, GFF, 32], I32)
    sgt_d = din("sgt", [128, I_FF], F32)
    sut_d = din("sut", [128, I_FF], F32)
    sdc_d = din("sdc", [GFF, HID], F32)
    zg_d = din("zg", [32, I_FF // 8], I32)
    zu_d = din("zu", [32, I_FF // 8], I32)
    zd_d = din("zd", [GFF, HID // 8], I32)

    out_d = nc.dram_tensor("out", [HID, T], F32, kind="ExternalOutput").ap()

    q_sp = nc.dram_tensor("q_sp", [128, NC * HID], BF16).ap()
    sg_sp = nc.dram_tensor("sg_sp", [128, NC * HID], BF16).ap()
    x1pre_sp = nc.dram_tensor("x1pre_sp", [HID, T], F32).ap()
    x1_sp = nc.dram_tensor("x1_sp", [HID, T], F32).ap()
    x1bf_sp = nc.dram_tensor("x1bf_sp", [HID, T], BF16).ap()
    h_sp = nc.dram_tensor("h_sp", [I_FF, T], BF16).ap()
    zs_sp = nc.dram_tensor("zs_sp", [64, I_FF], BF16).ap()
    cc_in = nc.dram_tensor("cc_in", [HID, D], F32).ap()
    cc_out = nc.dram_tensor("cc_out", [NCORE * HID, D], F32,
                            addr_space="Shared").ap()
    ccd_in = nc.dram_tensor("ccd_in", [1, H], F32).ap()
    ccd_out = nc.dram_tensor("ccd_out", [NCORE, H], F32,
                             addr_space="Shared").ap()

    dbg_outs = {}
    if dbg:
        for nm, shape, dt in [
            ("dbg_ktm", [128, NC * KVH * 128], BF16),
            ("dbg_vtm", [128, NC * KVH * 128], BF16),
            ("dbg_kfm", [128, KVH * T], BF16),
            ("dbg_dt", [128, NC * H], F32),
            ("dbg_G", [128, NC * H], F32),
            ("dbg_Sloc", [128, H * 128], F32),
            ("dbg_Sin", [128, H * 128], F32),
            ("dbg_yh", [128, H * T], BF16),
            ("dbg_x1", [HID, T], F32),
            ("dbg_h", [I_FF, T], BF16),
            ("dbg_st1", [1, 4 * T], F32),
            ("dbg_st2", [1, 4 * T], F32),
            ("dbg_pre2", [HID, T], BF16),
            ("dbg_w", [128, 2048], F32),
            ("dbg_wd", [128, 4096], BF16),
            ("dbg_hs", [GFF, T], BF16),
            ("dbg_zsd", [GFF, HID], BF16),
        ]:
            dbg_outs[nm] = nc.dram_tensor(nm, shape, dt,
                                          kind="ExternalOutput").ap()

    st = contextlib.ExitStack()
    nc._exit_stack = st

    def sbt(name, shape, dt, stack=None):
        return (stack or st).enter_context(nc.sbuf_tensor(name, shape, dt))

    big0 = sbt("big0", [128, KC * T], BF16)
    scr = sbt("scr", [128, 5 * T], F32)
    scrb = sbt("scrb", [128, 5632], BF16)
    x1ring = sbt("x1ring", [128, 2 * T], F32)
    c_tril = sbt("c_tril", [CH, CH], F32)
    c_triu = sbt("c_triu", [CH, CH], F32)
    c_idf = sbt("c_idf", [128, 128], F32)
    c_idb = sbt("c_idb", [128, 128], BF16)
    c_mask = sbt("c_mask", [CH, CH], BF16)
    c_o1f = sbt("c_o1f", [1, 128], F32)
    c_o1b = sbt("c_o1b", [1, 128], BF16)
    c_ocf = sbt("c_ocf", [128, 1], F32)
    c_ocb = sbt("c_ocb", [128, 1], BF16)
    c_mg = sbt("c_mg", [128, 32], BF16)
    c_oh = sbt("c_oh", [128, 171], BF16)
    c_gn = sbt("c_gn", [128, 1], F32)
    c_cm = sbt("c_cm", [1, NCORE], F32)
    c_cos = sbt("c_cos", [128, T], BF16)
    c_sin = sbt("c_sin", [128, T], BF16)
    c_dtbB = sbt("c_dtbB", [128, H], F32)
    c_AB = sbt("c_AB", [128, H], F32)
    r_alog = sbt("r_alog", [1, H], F32)
    r_dtb = sbt("r_dtb", [1, H], F32)
    c_cv = sbt("c_cv", [128, 8], F32)
    for _i, _val in enumerate([-1.0, -0.5, 1.0 / 128, 1e-5, 1.0 / HID]):
        nc.const_aps.register(F32, _val, c_cv[:, _i:_i + 1])

    stA = contextlib.ExitStack()
    k_tm = sbt("k_tm", [128, NC * KVH * 128], BF16, stA)
    v_tm = sbt("v_tm", [128, NC * KVH * 128], BF16, stA)
    k_fm = sbt("k_fm", [128, KVH * T], BF16, stA)
    S_f32 = sbt("S_f32", [128, H * 128], F32, stA)
    S_bf = sbt("S_bf", [128, H * 128], BF16, stA)
    dt_tm = sbt("dt_tm", [128, NC * H], F32, stA)
    G_tm = sbt("G_tm", [128, NC * H], F32, stA)
    eG = sbt("eG", [128, NC * H], F32, stA)
    eGr = sbt("eGr", [128, NC * H], F32, stA)
    decB = sbt("decB", [128, NC * H], F32, stA)
    eGl_row = sbt("eGl_row", [1, NC * H], F32, stA)
    Dp_row = sbt("Dp_row", [1, H], F32, stA)
    rowsm = sbt("rowsm", [NCORE, H], F32, stA)
    wstrip = sbt("wstrip", [128, 2 * 4096], BF16, stA)

    pf = st.enter_context(nc.psum_tensor("pf", [128, 3072], F32))
    pbA = st.enter_context(nc.psum_tensor("pbA", [128, 1024], BF16))
    pbB = st.enter_context(nc.psum_tensor("pbB", [128, 1024], BF16))

    S = Sched(nc, st)
    say, now = S.do, S.now
    S.sem("cc")

    def consts_load(e):
        last = None
        for dst, src in [(c_tril, trilm_d), (c_triu, triu_d), (c_idf, identf_d),
                         (c_idb, identb_d), (c_mask, maskT_d), (c_o1f, ones1f_d),
                         (c_o1b, ones1b_d), (c_ocf, onescf_d), (c_ocb, onescb_d),
                         (c_mg, mgsel_d), (c_oh, oh86_d), (c_gn, gncol_d),
                         (c_cm, cmask_d), (c_cos, cosf_d), (c_sin, sinf_d),
                         (r_alog, alog_d), (r_dtb, dtb_d),
                         (c_cv, cvals_d)]:
            last = e.dma_start(out=dst[:], in_=src[:])
        return last
    say("sync", consts_load, inc=("dq", 16 * 18))

    def xload(e):
        last = None
        for kc in range(KC):
            last = e.dma_start(out=big0[:, kc * T:(kc + 1) * T],
                               in_=xT[kc * 128:(kc + 1) * 128, :])
        return last
    say("gpsimd", xload, inc=("dx", 16 * KC))

    def pre_sc(e):
        e.activation(scr[0:1, 0:H], r_alog[:], AF.Exp)
        return e.mul(scr[0:1, 0:H], scr[0:1, 0:H], -1.0)
    say("scalar", pre_sc, waits=[("dq", now("dq"))], inc=("s", 1))

    def pre_mm(e):
        e.matmul(pf[0:128, 0:H], c_o1f[:], scr[0:1, 0:H], start=True, stop=True)
        return e.matmul(pf[0:128, 512:512 + H], c_o1f[:], r_dtb[:],
                        start=True, stop=True)
    say("tensor", pre_mm, waits=[("s", now("s"))], inc=("t", 1))

    def pre_ev(e):
        e.tensor_copy(c_AB[:], pf[0:128, 0:H])
        return e.tensor_copy(c_dtbB[:], pf[0:128, 512:512 + H])
    say("vector", pre_ev, waits=[("t", now("t"))], inc=("v", 1))
    preev_v = now("v")

    slab_pe = {}
    slab_ev = {}
    for nt in range(P2SLABS):
        slot = (nt % 2) * 4096
        pp = (nt % 2) * 1024

        def ld(e, nt=nt, slot=slot):
            return e.dma_start(out=wstrip[:, slot:slot + 4096], in_=wkv_d[nt])
        w = [("t", slab_pe[nt - 2])] if nt >= 2 else []
        say("gpsimd", ld, waits=w, inc=("wv%d" % (nt % 2), 16))
        wv_tgt = now("wv%d" % (nt % 2))

        def mm(e, nt=nt, slot=slot, pp=pp):
            last = None
            for kc in range(KC):
                for th in range(2):
                    last = e.matmul(
                        pf[0:128, pp + th * 512: pp + (th + 1) * 512],
                        wstrip[:, slot + kc * 128: slot + (kc + 1) * 128],
                        big0[:, kc * T + th * 512: kc * T + (th + 1) * 512],
                        start=(kc == 0), stop=(kc == KC - 1))
            return last
        w = [("wv%d" % (nt % 2), wv_tgt), ("dx", now("dx")),
             ("dq", now("dq"))]
        if nt >= 2:
            w.append(("v", slab_ev[nt - 2]))
        else:
            w.append(("v", preev_v))
        say("tensor", mm, waits=w, inc=("t", 1))
        slab_pe[nt] = now("t")

        def ev(e, nt=nt, pp=pp):
            if nt < 8:
                return e.tensor_copy(k_fm[:, nt * T:(nt + 1) * T],
                                     pf[0:128, pp:pp + 1024])
            if nt < 16:
                return e.tensor_copy(scrb[:, 0:T], pf[0:128, pp:pp + 1024])
            return e.tensor_copy(scr[0:128, 0:1024], pf[0:128, pp:pp + 1024])
        say("vector", ev, waits=[("t", slab_pe[nt])], inc=("v", 1))
        slab_ev[nt] = now("v")

        if nt < 8:
            def rope(e, nt=nt):
                sl = k_fm[:, nt * T:(nt + 1) * T]
                rot = scrb[:, T:2 * T]
                e.tensor_scalar(rot[0:64, :], sl[64:128, :], -1.0, None, OP.mult)
                e.tensor_copy(rot[64:128, :], sl[0:64, :])
                a = scrb[:, 2 * T:3 * T]
                e.tensor_tensor(a, sl, c_cos[:], OP.mult)
                e.tensor_tensor(rot, rot, c_sin[:], OP.mult)
                return e.tensor_tensor(sl, a, rot, OP.add)
            say("vector", rope, inc=("v", 1))
        src = (lambda c, nt=nt: k_fm[:, nt * T + c * 128: nt * T + (c + 1) * 128]) \
            if nt < 8 else (lambda c: scrb[:, c * 128:(c + 1) * 128])
        if nt < 16 and (P2PARTS & 2):
            dstbuf = k_tm if nt < 8 else v_tm
            kv = nt % 8
            for c in range(NC):
                def tr(e, c=c, src=src):
                    return e.transpose(pbA[:, 0:128], src(c), c_idb[:])
                say("tensor", tr, waits=[("v", now("v"))], inc=("t", 1))

                def trev(e, c=c, kv=kv, dstbuf=dstbuf):
                    return e.tensor_copy(
                        dstbuf[:, (c * KVH + kv) * 128:(c * KVH + kv + 1) * 128],
                        pbA[:, 0:128])
                say("vector", trev, waits=[("t", now("t"))], inc=("v", 1))
    slab_pe[17] = now("t")

    if not (P2PARTS & 4):
        if dbg:
            def d2a(e):
                e.dma_start(out=dbg_outs["dbg_ktm"][:], in_=k_tm[:])
                e.dma_start(out=dbg_outs["dbg_vtm"][:], in_=v_tm[:])
                return e.dma_start(out=dbg_outs["dbg_kfm"][:], in_=k_fm[:])
            say("sync", d2a, waits=[("v", now("v")), ("s", now("s"))],
                inc=("dq", 16 * 3))
        return _finish(nc, S, st, stA)

    def dtbT(e):
        return e.transpose(pf[0:32, 2048:2176], c_dtbB[:], c_idf[:])
    say("tensor", dtbT, waits=[("v", now("v"))], inc=("t", 1))

    def dt_col(e):
        e.memset(scr[0:128, 3072:3073], 0.0)
        return e.tensor_copy(scr[0:32, 3072:3073], pf[0:32, 2048:2049])
    say("vector", dt_col, waits=[("t", now("t"))], inc=("v", 1))
    say("scalar", lambda e: e.nop(), waits=[("v", now("v"))], inc=("s", 1))

    def dt_chain(e):
        p = scr[0:128, 0:1024]
        e.tensor_scalar(p, p, scr[0:128, 3072:3073], None, OP.add)
        return e.tensor_scalar(scr[0:128, 2048:3072], p, 0.0, None, OP.max)
    say("vector", dt_chain, waits=[("s", now("s"))], inc=("v", 1))

    def dt_sc(e):
        ap = scr[0:128, 1024:2048]
        e.activation(ap, scr[0:128, 0:1024], AF.Abs)
        e.activation(ap, ap, AF.Exp, scale=-1.0)
        return e.activation(ap, ap, AF.Ln, bias=1.0)
    say("scalar", dt_sc, waits=[("v", now("v"))], inc=("s", 1))

    def dt_fin(e):
        return e.tensor_tensor(scr[0:128, 0:1024], scr[0:128, 2048:3072],
                               scr[0:128, 1024:2048], OP.add)
    say("vector", dt_fin, waits=[("s", now("s"))], inc=("v", 1))

    for c in range(NC):
        def tr(e, c=c):
            return e.transpose(pf[0:128, 2048 + H:2048 + 2 * H],
                               scr[0:32, c * 128:(c + 1) * 128],
                               c_idf[0:32, 0:32])
        say("tensor", tr, waits=[("v", now("v"))], inc=("t", 1))

        def trev(e, c=c):
            return e.tensor_copy(dt_tm[:, c * H:(c + 1) * H],
                                 pf[0:128, 2048 + H:2048 + 2 * H])
        say("vector", trev, waits=[("t", now("t"))], inc=("v", 1))

    def gbuild(e):
        last = None
        for c in range(NC):
            last = e.tensor_tensor(scr[0:128, 1024 + c * H:1024 + (c + 1) * H],
                                   dt_tm[:, c * H:(c + 1) * H], c_AB[:], OP.mult)
        return last
    say("vector", gbuild, inc=("v", 1))

    for c in range(NC):
        def gmm(e, c=c):
            g = scr[0:128, 1024 + c * H:1024 + (c + 1) * H]
            e.matmul(pf[0:128, 2048:2048 + H], c_tril[:], g, start=True, stop=True)
            e.matmul(pf[0:128, 2560:2560 + H], c_triu[:], g, start=True, stop=True)
            return e.matmul(pf[0:1, 2560 + H:2560 + 2 * H], c_ocf[:], g,
                            start=True, stop=True)
        say("tensor", gmm, waits=[("v", now("v"))], inc=("t", 1))

        def gev(e, c=c):
            e.tensor_copy(G_tm[:, c * H:(c + 1) * H], pf[0:128, 2048:2048 + H])
            e.tensor_copy(eGr[:, c * H:(c + 1) * H], pf[0:128, 2560:2560 + H])
            return e.tensor_copy(eGl_row[0:1, c * H:(c + 1) * H],
                                 pf[0:1, 2560 + H:2560 + 2 * H])
        say("vector", gev, waits=[("t", now("t"))], inc=("v", 1))

    def gexp(e):
        e.activation(eG[:], G_tm[:], AF.Exp)
        e.activation(eGr[:], eGr[:], AF.Exp)
        return e.activation(eGl_row[:], eGl_row[:], AF.Exp)
    say("scalar", gexp, waits=[("v", now("v"))], inc=("s", 1))

    for c in range(NC):
        def dmm(e, c=c):
            return e.matmul(pf[0:128, 2048:2048 + H], c_o1f[:],
                            eGl_row[0:1, c * H:(c + 1) * H], start=True, stop=True)
        say("tensor", dmm, waits=[("s", now("s")), ("v", now("v"))], inc=("t", 1))

        def dev(e, c=c):
            return e.tensor_copy(decB[:, c * H:(c + 1) * H],
                                 pf[0:128, 2048:2048 + H])
        say("vector", dev, waits=[("t", now("t"))], inc=("v", 1))

    if dbg:
        def d2(e):
            e.dma_start(out=dbg_outs["dbg_Sloc"][0:128, 0:H], in_=c_dtbB[:])
            e.dma_start(out=dbg_outs["dbg_Sloc"][0:128, H:2 * H], in_=c_AB[:])
            e.dma_start(out=dbg_outs["dbg_Sloc"][0:128, 2 * H:2 * H + 1],
                        in_=scr[0:128, 3072:3073],
                        allow_slow_non_contiguous=True)
            e.dma_start(out=dbg_outs["dbg_ktm"][:], in_=k_tm[:])
            e.dma_start(out=dbg_outs["dbg_vtm"][:], in_=v_tm[:])
            e.dma_start(out=dbg_outs["dbg_kfm"][:], in_=k_fm[:])
            e.dma_start(out=dbg_outs["dbg_dt"][:], in_=dt_tm[:])
            return e.dma_start(out=dbg_outs["dbg_G"][:], in_=G_tm[:])
        say("sync", d2, waits=[("v", now("v")), ("s", now("s"))],
            inc=("dq", 16 * 8))
    if stop_after <= 2:
        return _finish(nc, S, st, stA)

    for c in range(NC):
        for hh in range(H):
            kv = hh // 4

            def mkop(e, c=c, hh=hh, kv=kv):
                vsl = v_tm[:, (c * KVH + kv) * 128:(c * KVH + kv + 1) * 128]
                ksl = k_tm[:, (c * KVH + kv) * 128:(c * KVH + kv + 1) * 128]
                e.tensor_scalar(scrb[:, 0:128], vsl,
                                dt_tm[:, c * H + hh: c * H + hh + 1], None,
                                OP.mult)
                return e.tensor_scalar(scrb[:, 128:256], ksl,
                                       eGr[:, c * H + hh: c * H + hh + 1], None,
                                       OP.mult)
            say("vector", mkop, waits=[("t", now("t"))], inc=("v", 1))

            def cmm(e):
                return e.matmul(pf[0:128, 2048:2176], scrb[:, 128:256],
                                scrb[:, 0:128], start=True, stop=True)
            say("tensor", cmm, waits=[("v", now("v"))], inc=("t", 1))

            def supd(e, c=c, hh=hh):
                ssl = S_f32[:, hh * 128:(hh + 1) * 128]
                if c == 0:
                    return e.tensor_copy(ssl, pf[0:128, 2048:2176])
                return e.scalar_tensor_tensor(
                    ssl, ssl, decB[:, c * H + hh: c * H + hh + 1],
                    pf[0:128, 2048:2176], OP.mult, OP.add)
            say("vector", supd, waits=[("t", now("t"))], inc=("v", 1))

    def dprod(e):
        e.memset(Dp_row[:], 1.0)
        last = None
        for c in range(NC):
            last = e.tensor_tensor(Dp_row[:], Dp_row[:],
                                   eGl_row[0:1, c * H:(c + 1) * H], OP.mult)
        return last
    say("vector", dprod, inc=("v", 1))

    if dbg:
        def d3(e):
            return e.dma_start(out=dbg_outs["dbg_Sloc"][:], in_=S_f32[:])
        say("sync", d3, waits=[("v", now("v"))], inc=("dq", 16))

    def pack(e):
        e.dma_start(out=cc_in.rearrange("(h d) e -> d h e", d=128),
                    in_=S_f32[:].rearrange("p (h e) -> p h e", h=H))
        return e.dma_start(out=ccd_in[:], in_=Dp_row[:])
    say("sync", pack, waits=[("v", now("v"))], inc=("dq", 32))

    ccsem = S.sems["cc"]

    if SKIP_CC:
        def coll(e):
            for m in range(NCORE):
                e.dma_start(out=cc_out[m * HID:(m + 1) * HID, :], in_=cc_in[:])
                e.dma_start(out=ccd_out[m:m + 1, :], in_=ccd_in[:])
            return e.dma_start(out=rowsm[:], in_=ccd_out[:])
        say("gpsimd", coll, waits=[("dq", now("dq"))], inc=("dg", 16 * 17))
    else:
        def coll(e):
            e.collective_compute("AllGather", OP.bypass,
                                 replica_groups=[list(range(NCORE))],
                                 ins=[cc_in], outs=[cc_out]).then_inc(ccsem, 1)
            e.wait_ge(ccsem, 1)
            e.collective_compute("AllGather", OP.bypass,
                                 replica_groups=[list(range(NCORE))],
                                 ins=[ccd_in], outs=[ccd_out]).then_inc(ccsem, 1)
            e.wait_ge(ccsem, 2)
            return e.dma_start(out=rowsm[:], in_=ccd_out[:])
        say("gpsimd", coll, waits=[("dq", now("dq"))], inc=("dg", 16))

    def dflat(e):
        return e.dma_start(out=scr[0:1, 0:NCORE * H], in_=rowsm[:])
    say("scalar", dflat, waits=[("dg", now("dg"))], inc=("ds", 16))

    def kcoef(e):
        Df = scr[0:1, 0:NCORE * H]
        Dm = scr[0:1, 1280:1280 + NCORE * H]
        tmp = scr[0:1, 1024 + 512:1024 + 512 + H]
        for m in range(NCORE):
            e.tensor_scalar(tmp, Df[0:1, m * H:(m + 1) * H],
                            -1.0, None, OP.add)
            e.tensor_scalar(Dm[0:1, m * H:(m + 1) * H], tmp,
                            c_cm[0:1, m:m + 1], 1.0, OP.mult, OP.add)
        R = scr[0:1, 1024 + 512:1024 + 512 + H]
        e.memset(R, 1.0)
        Kf = scr[0:1, 2048:2048 + NCORE * H]
        last = None
        for m in range(NCORE - 1, -1, -1):
            last = e.tensor_scalar(Kf[0:1, m * H:(m + 1) * H], R,
                                   c_cm[0:1, m:m + 1], None, OP.mult)
            last = e.tensor_tensor(R, R, Dm[0:1, m * H:(m + 1) * H], OP.mult)
        return last
    say("vector", kcoef, waits=[("ds", now("ds"))], inc=("v", 1))

    def kbc(e):
        return e.matmul(pf[0:128, 2048:2304], c_o1f[:], scr[0:1, 2048:2304],
                        start=True, stop=True)
    say("tensor", kbc, waits=[("v", now("v"))], inc=("t", 1))

    def kbev(e):
        return e.tensor_copy(scr[0:128, 4096:4352], pf[0:128, 2048:2304])
    say("vector", kbev, waits=[("t", now("t"))], inc=("v", 1))

    for m in range(NCORE):
        def gld(e, m=m):
            return e.dma_start(
                out=scr[0:128, 0:4096].rearrange("p (h e) -> p h e", h=H),
                in_=cc_out[m * HID:(m + 1) * HID, :].rearrange(
                    "(h d) e -> d h e", d=128))
        say("sync", gld, waits=[("v", now("v")), ("dg", now("dg"))],
            inc=("dq", 16))

        def sacc(e, m=m):
            last = None
            for hh in range(H):
                kb = scr[0:128, 4096 + m * H + hh: 4096 + m * H + hh + 1]
                tile = scr[0:128, hh * 128:(hh + 1) * 128]
                ssl = S_f32[:, hh * 128:(hh + 1) * 128]
                if m == 0:
                    last = e.tensor_scalar(ssl, tile, kb, None, OP.mult)
                else:
                    last = e.scalar_tensor_tensor(ssl, tile, kb, ssl,
                                                  OP.mult, OP.add)
            return last
        say("vector", sacc, waits=[("dq", now("dq"))], inc=("v", 1))

    def scast(e):
        return e.tensor_copy(S_bf[:], S_f32[:])
    say("vector", scast, inc=("v", 1))

    if dbg:
        def d4(e):
            return e.dma_start(out=dbg_outs["dbg_Sin"][:], in_=S_f32[:])
        say("sync", d4, waits=[("v", now("v"))], inc=("dq", 16))
    if stop_after <= 4:
        return _finish(nc, S, st, stA)

    p5rec = {}
    p5i = 0
    for which, wd_, spd in [("q", wq_d, q_sp), ("g", wg_d, sg_sp)]:
        for nt in range(KC):
            slot = (nt % 2) * 4096
            pp = (nt % 2) * 1024

            def ld(e, nt=nt, slot=slot, wd_=wd_):
                return e.dma_start(out=wstrip[:, slot:slot + 4096], in_=wd_[nt])
            say("gpsimd", ld,
                waits=[("t", p5rec.get(p5i - 2, now("t")))],
                inc=("wv%d" % (p5i % 2), 16))
            wv_tgt = now("wv%d" % (p5i % 2))

            def mm(e, slot=slot, pp=pp):
                last = None
                for kc in range(KC):
                    for th in range(2):
                        last = e.matmul(
                            pf[0:128, pp + th * 512: pp + (th + 1) * 512],
                            wstrip[:, slot + kc * 128: slot + (kc + 1) * 128],
                            big0[:, kc * T + th * 512: kc * T + (th + 1) * 512],
                            start=(kc == 0), stop=(kc == KC - 1))
                return last
            say("tensor", mm,
                waits=[("wv%d" % (p5i % 2), wv_tgt), ("v", now("v"))],
                inc=("t", 1))
            p5rec[p5i] = now("t")
            p5i += 1

            if which == "q":
                def ev(e, pp=pp):
                    sl = scrb[:, 0:T]
                    e.tensor_copy(sl, pf[0:128, pp:pp + 1024])
                    rot = scrb[:, T:2 * T]
                    e.tensor_scalar(rot[0:64, :], sl[64:128, :], -1.0, None,
                                    OP.mult)
                    e.tensor_copy(rot[64:128, :], sl[0:64, :])
                    a = scrb[:, 2 * T:3 * T]
                    e.tensor_tensor(a, sl, c_cos[:], OP.mult)
                    e.tensor_tensor(rot, rot, c_sin[:], OP.mult)
                    return e.tensor_tensor(sl, a, rot, OP.add)
                say("vector", ev, waits=[("t", now("t"))], inc=("v", 1))

                def sp(e, nt=nt, spd=spd):
                    return e.dma_start(
                        out=spd.rearrange("p (c r) -> p c r", r=HID)[
                            :, :, nt * 128:(nt + 1) * 128],
                        in_=scrb[:, 0:T].rearrange("p (c i) -> p c i", c=NC))
                say("sync", sp, waits=[("v", now("v"))], inc=("sw", 16))
            else:
                def ev(e, pp=pp):
                    return e.activation(scrb[:, 0:T], pf[0:128, pp:pp + 1024],
                                        AF.Silu)
                say("scalar", ev, waits=[("t", now("t")), ("v", now("v"))],
                    inc=("s", 1))

                def gsc(e):
                    return e.tensor_scalar(scrb[:, 0:T], scrb[:, 0:T],
                                           c_gn[:], None, OP.mult)
                say("vector", gsc, waits=[("s", now("s"))], inc=("v", 1))
                for c in range(NC):
                    def tr(e, c=c):
                        return e.transpose(pbA[:, 0:128],
                                           scrb[:, c * 128:(c + 1) * 128],
                                           c_idb[:])
                    say("tensor", tr, waits=[("v", now("v"))], inc=("t", 1))

                    def trev(e, c=c):
                        return e.tensor_copy(
                            scrb[:, T + c * 128:T + (c + 1) * 128], pbA[:, 0:128])
                    say("vector", trev, waits=[("t", now("t"))], inc=("v", 1))

                def sp(e, nt=nt, spd=spd):
                    return e.dma_start(
                        out=spd.rearrange("p (c r) -> p c r", r=HID)[
                            :, :, nt * 128:(nt + 1) * 128],
                        in_=scrb[:, T:2 * T].rearrange("p (c i) -> p c i", c=NC))
                say("sync", sp, waits=[("v", now("v"))], inc=("sw", 16))

    qch = wstrip[:, 0:4096]
    sgch = wstrip[:, 4096:8192]
    def pr(hh, idx):
        base = 2048 if (hh % 2 == 0) else 2560
        return pf[0:128, base + idx * 128: base + (idx + 1) * 128]
    rec = {}
    for c in range(NC):
        def qld(e, c=c):
            return e.dma_start(out=qch, in_=q_sp[:, c * HID:(c + 1) * HID])
        w = [("t", now("t"))]
        if c == 0:
            w.append(("sw", now("sw")))
        say("sync", qld, waits=w, inc=("qc", 16))

        def sgld(e, c=c):
            return e.dma_start(out=sgch, in_=sg_sp[:, c * HID:(c + 1) * HID])
        say("sync", sgld, waits=[("v", now("v"))], inc=("sgc", 16))
        qc_tgt = now("qc")
        sgc_tgt = now("sgc")

        def gt(e, c=c):
            return e.transpose(pf[0:32, 2048:2176], G_tm[:, c * H:(c + 1) * H],
                               c_idf[:])
        say("tensor", gt, waits=[("v", now("v"))], inc=("t", 1))

        def gtev(e):
            return e.tensor_copy(scr[0:32, 4864:4992], pf[0:32, 2048:2176])
        say("vector", gtev, waits=[("t", now("t"))], inc=("v", 1))

        def gfl(e):
            return e.dma_start(out=scr[0:1, 0:4096], in_=scr[0:32, 4864:4992])
        say("scalar", gfl, waits=[("v", now("v"))], inc=("gf", 16))
        dsv = now("gf")

        for hh in range(H):
            kv = hh // 4
            par = hh % 2
            gslot = ((hh // 8) % 2) * 1024
            if hh % 8 == 0:
                def gib(e, hh=hh, gslot=gslot):
                    last = None
                    for k in range(8):
                        last = e.matmul(
                            pf[0:128, gslot + k * 128: gslot + (k + 1) * 128],
                            c_o1f[:],
                            scr[0:1, (hh + k) * 128:(hh + k + 1) * 128],
                            start=True, stop=True)
                    return last
                say("tensor", gib, waits=[("gf", dsv), ("v", now("v"))],
                    inc=("t", 1))

            xb = 3072 + par * 1280
            sb_ = 4096 + par * 512

            def va(e, c=c, hh=hh, kv=kv, xb=xb):
                vsl = v_tm[:, (c * KVH + kv) * 128:(c * KVH + kv + 1) * 128]
                ksl = k_tm[:, (c * KVH + kv) * 128:(c * KVH + kv + 1) * 128]
                e.tensor_scalar(scrb[:, xb:xb + 128], vsl,
                                dt_tm[:, c * H + hh: c * H + hh + 1], None,
                                OP.mult)
                return e.tensor_scalar(scrb[:, xb + 128:xb + 256], ksl,
                                       eGr[:, c * H + hh: c * H + hh + 1], None,
                                       OP.mult)
            w = [("t", rec[hh - 2][0])] if hh >= 2 else [("t", now("t"))]
            say("vector", va, waits=w, inc=("v", 1))
            v_va = now("v")

            def peb(e, c=c, hh=hh, kv=kv):
                return e.matmul(pr(hh, 0),
                                k_fm[:, kv * T + c * 128: kv * T + (c + 1) * 128],
                                qch[:, hh * 128:(hh + 1) * 128],
                                start=True, stop=True)
            wpeb = [("v", v_va)]
            if hh == 0:
                wpeb.append(("qc", qc_tgt))
            say("tensor", peb, waits=wpeb, inc=("t", 1))
            t_peb = now("t")

            def vc(e, c=c, hh=hh, gslot=gslot, sb_=sb_):
                gsl = pf[0:128, gslot + (hh % 8) * 128: gslot + (hh % 8 + 1) * 128]
                return e.tensor_scalar(scr[0:128, sb_:sb_ + 128], gsl,
                                       G_tm[:, c * H + hh: c * H + hh + 1],
                                       0.0, OP.subtract, OP.min)
            say("vector", vc, waits=[("t", t_peb)], inc=("v", 1))

            def sd_(e, xb=xb, sb_=sb_):
                return e.activation(scrb[:, xb + 256:xb + 384],
                                    scr[0:128, sb_:sb_ + 128], AF.Exp)
            say("scalar", sd_, waits=[("v", now("v"))], inc=("s", 1))

            def ve(e, hh=hh, xb=xb):
                dsl = scrb[:, xb + 256:xb + 384]
                e.tensor_tensor(dsl, dsl, c_mask[:], OP.mult)
                return e.tensor_tensor(scrb[:, xb + 384:xb + 512],
                                       pr(hh, 0), dsl, OP.mult)
            say("vector", ve, waits=[("s", now("s"))], inc=("v", 1))

            def pef(e, c=c, hh=hh, xb=xb):
                e.matmul(pr(hh, 1), scrb[:, xb + 384:xb + 512],
                         scrb[:, xb:xb + 128], start=True, stop=True)
                e.matmul(pr(hh, 2), qch[:, hh * 128:(hh + 1) * 128],
                         S_bf[:, hh * 128:(hh + 1) * 128], start=True, stop=True)
                return e.matmul(pr(hh, 3), scrb[:, xb + 128:xb + 256],
                                scrb[:, xb:xb + 128], start=True, stop=True)
            say("tensor", pef, waits=[("v", now("v"))], inc=("t", 1))
            t_pef = now("t")

            def vg(e, c=c, hh=hh, sb_=sb_):
                y = scr[0:128, sb_ + 128:sb_ + 256]
                e.tensor_scalar(y, pr(hh, 2),
                                eG[:, c * H + hh: c * H + hh + 1], None, OP.mult)
                e.tensor_tensor(y, y, pr(hh, 1), OP.add)
                ysq = scr[0:128, sb_ + 256:sb_ + 384]
                e.tensor_tensor(ysq, y, y, OP.mult)
                e.tensor_reduce(scr[0:128, sb_ + 384:sb_ + 385], ysq, AX.X, OP.add)
                ssl = S_f32[:, hh * 128:(hh + 1) * 128]
                e.scalar_tensor_tensor(ssl, ssl,
                                       decB[:, c * H + hh: c * H + hh + 1],
                                       pr(hh, 3), OP.mult, OP.add)
                return e.tensor_copy(S_bf[:, hh * 128:(hh + 1) * 128], ssl)
            say("vector", vg, waits=[("t", t_pef)], inc=("v", 1))

            def sh(e, sb_=sb_):
                r = scr[0:128, sb_ + 384:sb_ + 385]
                e.activation(r, r, AF.Ln, scale=1.0 / D, bias=RMS_EPS)
                return e.activation(r, r, AF.Exp, scale=-0.5)
            say("scalar", sh, waits=[("v", now("v"))], inc=("s", 1))

            def vi(e, hh=hh, xb=xb, sb_=sb_):
                y = scr[0:128, sb_ + 128:sb_ + 256]
                yn = scrb[:, xb + 512:xb + 640]
                e.tensor_scalar(yn, y, scr[0:128, sb_ + 384:sb_ + 385], None,
                                OP.mult)
                return e.tensor_tensor(yn, yn, sgch[:, hh * 128:(hh + 1) * 128],
                                       OP.mult)
            wvi = [("s", now("s"))]
            if hh == 0:
                wvi.append(("sgc", sgc_tgt))
            say("vector", vi, waits=wvi, inc=("v", 1))

            pbt = pbA if par == 0 else pbB

            def pj(e, xb=xb, pbt=pbt):
                return e.transpose(pbt[:, 0:128], scrb[:, xb + 512:xb + 640],
                                   c_idb[:])
            say("tensor", pj, waits=[("v", now("v"))], inc=("t", 1))
            t_pj = now("t")

            def vk(e, c=c, hh=hh, pbt=pbt):
                return e.tensor_copy(
                    big0[:, hh * T + c * 128: hh * T + (c + 1) * 128],
                    pbt[:, 0:128])
            say("vector", vk, waits=[("t", t_pj)], inc=("v", 1))
            rec[hh] = (t_pj, now("v"))

    if dbg:
        def d6(e):
            return e.dma_start(out=dbg_outs["dbg_yh"][:], in_=big0[:])
        say("sync", d6, waits=[("v", now("v"))], inc=("dq", 16))
    if stop_after <= 6:
        return _finish(nc, S, st, stA)

    def stat0(e):
        e.memset(scr[0:1, 3072:4096], 0.0)
        return e.memset(scr[0:1, 4096:5120], 0.0)
    say("vector", stat0, inc=("v", 1))

    p7rec = {}
    for ot in range(KC):
        slot = (ot % 2) * 4096
        pp = (ot % 2) * 1024

        def ld(e, ot=ot, slot=slot):
            return e.dma_start(out=wstrip[:, slot:slot + 4096], in_=wo_d[ot])
        say("gpsimd", ld, waits=[("t", p7rec.get(ot - 2, now("t")))],
            inc=("wv%d" % (ot % 2), 16))
        wv_tgt = now("wv%d" % (ot % 2))

        def xrl(e, ot=ot):
            return e.dma_start(out=x1ring[:, (ot % 2) * T:(ot % 2 + 1) * T],
                               in_=xT[ot * 128:(ot + 1) * 128, :])
        w = [("v", now("v"))] if ot >= 2 else []
        say("sync", xrl, waits=w, inc=("xr%d" % (ot % 2), 16))
        xr_tgt = now("xr%d" % (ot % 2))

        def mm(e, slot=slot, pp=pp):
            last = None
            for kc in range(KC):
                for th in range(2):
                    last = e.matmul(
                        pf[0:128, pp + th * 512: pp + (th + 1) * 512],
                        wstrip[:, slot + kc * 128: slot + (kc + 1) * 128],
                        big0[:, kc * T + th * 512: kc * T + (th + 1) * 512],
                        start=(kc == 0), stop=(kc == KC - 1))
            return last
        say("tensor", mm,
            waits=[("wv%d" % (ot % 2), wv_tgt), ("v", now("v"))],
            inc=("t", 1))
        p7rec[ot] = now("t")

        def ev(e, ot=ot, pp=pp):
            xp = scr[0:128, 0:T]
            e.tensor_tensor(xp, pf[0:128, pp:pp + 1024],
                            x1ring[:, (ot % 2) * T:(ot % 2 + 1) * T], OP.add)
            xbb = scrb[:, 0:T]
            e.tensor_copy(xbb, xp)
            return e.tensor_tensor(scrb[:, T:2 * T], xbb, xbb, OP.mult)
        say("vector", ev,
            waits=[("t", now("t")), ("xr%d" % (ot % 2), xr_tgt)],
            inc=("v", 1))

        def spl(e, ot=ot):
            return e.dma_start(out=x1pre_sp[ot * 128:(ot + 1) * 128, :],
                               in_=scr[0:128, 0:T])
        say("sync", spl, waits=[("v", now("v"))], inc=("xp", 16))

        for th in range(2):
            def smm(e, th=th):
                e.matmul(pf[0:1, 2048:2560], c_ocb[:],
                         scrb[:, th * 512:(th + 1) * 512], start=True, stop=True)
                return e.matmul(pf[0:1, 2560:3072], c_ocb[:],
                                scrb[:, T + th * 512:T + (th + 1) * 512],
                                start=True, stop=True)
            say("tensor", smm, waits=[("v", now("v"))], inc=("t", 1))

            def sac(e, th=th):
                e.tensor_tensor(scr[0:1, 3072 + th * 512:3072 + (th + 1) * 512],
                                scr[0:1, 3072 + th * 512:3072 + (th + 1) * 512],
                                pf[0:1, 2048:2560], OP.add)
                return e.tensor_tensor(
                    scr[0:1, 4096 + th * 512:4096 + (th + 1) * 512],
                    scr[0:1, 4096 + th * 512:4096 + (th + 1) * 512],
                    pf[0:1, 2560:3072], OP.add)
            say("vector", sac, waits=[("t", now("t"))], inc=("v", 1))

    def ln1fin(e):
        mu = scr[0:1, 3072:4096]
        e.tensor_scalar(mu, mu, 1.0 / HID, None, OP.mult)
        ex2 = scr[0:1, 4096:5120]
        e.tensor_scalar(ex2, ex2, 1.0 / HID, None, OP.mult)
        mu2 = scr[0:1, 2048:3072]
        e.tensor_tensor(mu2, mu, mu, OP.mult)
        return e.tensor_tensor(ex2, ex2, mu2, OP.subtract)
    say("vector", ln1fin, inc=("v", 1))

    def ln1rs(e):
        v_ = scr[0:1, 4096:5120]
        e.activation(v_, v_, AF.Ln, bias=LN_EPS)
        return e.activation(v_, v_, AF.Exp, scale=-0.5)
    say("scalar", ln1rs, waits=[("v", now("v"))], inc=("s", 1))

    def nmr(e):
        nm = scr[0:1, 3072:4096]
        e.tensor_tensor(nm, nm, scr[0:1, 4096:5120], OP.mult)
        return e.tensor_scalar(nm, nm, -1.0, None, OP.mult)
    say("vector", nmr, waits=[("s", now("s"))], inc=("v", 1))

    if dbg:
        def dst1(e):
            e.dma_start(out=dbg_outs["dbg_st1"][0:1, 0:1024],
                        in_=scr[0:1, 3072:4096])
            return e.dma_start(out=dbg_outs["dbg_st1"][0:1, 1024:2048],
                               in_=scr[0:1, 4096:5120])
        say("sync", dst1, waits=[("s", now("s")), ("v", now("v"))],
            inc=("dq", 32))

    def bcmm(e):
        e.matmul(pf[0:128, 2048:2560], c_o1f[:], scr[0:1, 4096:4608],
                 start=True, stop=True)
        e.matmul(pf[0:128, 2560:3072], c_o1f[:], scr[0:1, 4608:5120],
                 start=True, stop=True)
        e.matmul(pf[0:128, 0:512], c_o1f[:], scr[0:1, 3072:3584],
                 start=True, stop=True)
        return e.matmul(pf[0:128, 512:1024], c_o1f[:], scr[0:1, 3584:4096],
                        start=True, stop=True)
    say("tensor", bcmm, waits=[("v", now("v"))], inc=("t", 1))

    def bcev(e):
        e.tensor_copy(scr[0:128, 0:T], pf[0:128, 2048:3072])
        return e.tensor_copy(scr[0:128, T:2 * T], pf[0:128, 0:1024])
    say("vector", bcev, waits=[("t", now("t"))], inc=("v", 1))

    for ch in range(KC):
        def x1l(e, ch=ch):
            return e.dma_start(out=x1ring[:, (ch % 2) * T:(ch % 2 + 1) * T],
                               in_=x1pre_sp[ch * 128:(ch + 1) * 128, :])
        w = [("v", now("v"))]
        if ch == 0:
            w.append(("xp", now("xp")))
        say("sync", x1l, waits=w, inc=("xr%d" % (ch % 2), 16))
        xr_tgt = now("xr%d" % (ch % 2))

        def ap_(e, ch=ch):
            xp = x1ring[:, (ch % 2) * T:(ch % 2 + 1) * T]
            t1 = scr[0:128, 2 * T:3 * T]
            e.tensor_tensor(t1, xp, scr[0:128, 0:T], OP.mult)
            e.tensor_tensor(t1, t1, scr[0:128, T:2 * T], OP.add)
            return e.tensor_copy(scrb[:, 0:T], t1)
        say("vector", ap_, waits=[("xr%d" % (ch % 2), xr_tgt),
                                  ("xw", now("xw"))], inc=("v", 1))

        def x1s(e, ch=ch):
            e.dma_start(out=x1_sp[ch * 128:(ch + 1) * 128, :],
                        in_=scr[0:128, 2 * T:3 * T])
            return e.dma_start(out=x1bf_sp[ch * 128:(ch + 1) * 128, :],
                               in_=scrb[:, 0:T])
        say("sync", x1s, waits=[("v", now("v"))], inc=("xw", 32))

    if dbg:
        def d8(e):
            last = None
            for ch in range(KC):
                last = e.dma_start(
                    out=dbg_outs["dbg_x1"][ch * 128:(ch + 1) * 128, :],
                    in_=x1_sp[ch * 128:(ch + 1) * 128, :])
            return last
        say("sync", d8, waits=[("dq", now("dq"))], inc=("dq", 16 * KC))
    if stop_after <= 8:
        return _finish(nc, S, st, stA)

    stA.close()
    stB = contextlib.ExitStack()
    qwbuf = sbt("qwbuf", [128, KC * 32], I32, stB)
    qwdbuf = sbt("qwdbuf", [128, GFF * 32], I32, stB)
    stile = sbt("stile", [128, 4 * T], BF16, stB)
    wdq = sbt("wdq", [128, 2 * T], BF16, stB)
    wdq2 = sbt("wdq2", [128, 2 * 256], BF16, stB)
    unp = sbt("unp", [128, 2 * 128], I32, stB)
    unpd = sbt("unpd", [128, 2 * 32], I32, stB)
    zswin = sbt("zswin", [32, 2 * 2816], BF16, stB)
    zs_d = sbt("zs_d", [GFF, HID], BF16, stB)
    sd_bf = sbt("sd_bf", [GFF, HID], BF16, stB)
    xsum_bf = sbt("xsum_bf", [32, T], BF16, stB)
    hsum_f = sbt("hsum_f", [GFF, T], F32, stB)
    hsum_bf = sbt("hsum_bf", [GFF, T], BF16, stB)
    hring = sbt("hring", [128, 4 * T], BF16, stB)
    sgb = sbt("sgb", [128, 2 * T], BF16, stB)
    rowb = sbt("rowb", [1, 22 * 256], BF16, stB)

    def x1pl(e):
        last = None
        for j in range(KC):
            last = e.dma_start(
                out=big0[:, j * T:(j + 1) * T],
                in_=x1bf_sp.rearrange("(b m) t -> b m t", m=128)[
                    :, 4 * j:4 * j + 4, :])
        return last
    say("sync", x1pl, waits=[("dq", now("dq")), ("v", now("v"))],
        inc=("xq", 16 * KC))

    def xsmm(e):
        last = None
        for j in range(KC):
            for th in range(2):
                last = e.matmul(pf[0:32, 2048 + th * 512:2560 + th * 512],
                                c_mg[:],
                                big0[:, j * T + th * 512: j * T + (th + 1) * 512],
                                start=(j == 0), stop=(j == KC - 1))
        return last
    say("tensor", xsmm, waits=[("xq", now("xq")), ("v", now("v"))],
        inc=("t", 1))

    def xsev(e):
        return e.tensor_copy(xsum_bf[:], pf[0:32, 2048:3072])
    say("vector", xsev, waits=[("t", now("t"))], inc=("v", 1))

    for mi, (zdram, sdram) in enumerate([(zg_d, sgt_d), (zu_d, sut_d)]):
        for half in range(2):
            cw = I_FF // 16 // 2 * 2
            w0 = half * 688

            def zl(e, zdram=zdram, w0=w0):
                return e.dma_start(out=scr[0:32, 0:688].bitcast(I32)[:, 0:688],
                                   in_=zdram[:, w0:w0 + 688])
            say("sync", zl, waits=[("v", now("v"))], inc=("zl", 16))

            def sl_(e, sdram=sdram, half=half):
                return e.dma_start(
                    out=scrb[0:32, 0:5504],
                    in_=sdram.rearrange("(g rr) n -> g rr n", rr=4)[
                        :, 0:1, half * 5504:(half + 1) * 5504])
            say("gpsimd", sl_, waits=[("v", now("v"))], inc=("sl", 16))

            def zunp(e):
                src = scr[0:32, 0:688].bitcast(I32)[:, 0:688]
                lo = scr[0:32, 704:1392].bitcast(I32)[:, 0:688]
                hi = scr[0:32, 1408:2096].bitcast(I32)[:, 0:688]
                e.tensor_scalar(lo, src, 0x0F0F0F0F, None, OP.bitwise_and)
                return e.tensor_scalar(hi, src, 4, 0x0F0F0F0F,
                                       OP.logical_shift_right, OP.bitwise_and)
            say("vector", zunp, waits=[("zl", now("zl"))], inc=("v", 1))

            def zcv(e):
                lo8 = scr[0:32, 704:1392].bitcast(I8)
                hi8 = scr[0:32, 1408:2096].bitcast(I8)
                zdst = scr[0:32, 2112:4864].bitcast(BF16)
                e.activation(
                    zdst.rearrange("p (g cc) -> p g cc", cc=128)[:, :, 0:64],
                    lo8.rearrange("p (g cc) -> p g cc", cc=64), AF.Copy)
                return e.activation(
                    zdst.rearrange("p (g cc) -> p g cc", cc=128)[:, :, 64:128],
                    hi8.rearrange("p (g cc) -> p g cc", cc=64), AF.Copy)
            say("scalar", zcv, waits=[("v", now("v"))], inc=("s", 1))

            def zmul(e):
                zdst = scr[0:32, 2112:4864].bitcast(BF16)
                return e.scalar_tensor_tensor(zdst, zdst, -1.0,
                                              scrb[0:32, 0:5504],
                                              OP.mult, OP.mult)
            say("vector", zmul, waits=[("s", now("s")), ("sl", now("sl"))],
                inc=("v", 1))

            def zsp(e, mi=mi, half=half):
                return e.dma_start(
                    out=zs_sp[mi * 32:(mi + 1) * 32,
                              half * 5504:(half + 1) * 5504],
                    in_=scr[0:32, 2112:4864].bitcast(BF16))
            say("sync", zsp, waits=[("v", now("v"))], inc=("zpw", 16))

    say("sync", lambda e: e.dma_start(
        out=scr[0:86, 0:512].bitcast(I32)[:, 0:512], in_=zd_d[:]),
        waits=[("v", now("v"))], inc=("zdl", 16))
    say("gpsimd", lambda e: e.dma_start(out=sd_bf[:], in_=sdc_d[:]),
        inc=("sdl", 16))

    def zdunp(e):
        src = scr[0:86, 0:512].bitcast(I32)[:, 0:512]
        lo = scr[0:86, 1024:1536].bitcast(I32)[:, 0:512]
        hi = scr[0:86, 2048:2560].bitcast(I32)[:, 0:512]
        e.tensor_scalar(lo, src, 0x0F0F0F0F, None, OP.bitwise_and)
        return e.tensor_scalar(hi, src, 4, 0x0F0F0F0F,
                               OP.logical_shift_right, OP.bitwise_and)
    say("vector", zdunp, waits=[("zdl", now("zdl"))], inc=("v", 1))

    def zdcv(e):
        lo8 = scr[0:86, 1024:1536].bitcast(I8)
        hi8 = scr[0:86, 2048:2560].bitcast(I8)
        e.activation(zs_d[:].rearrange("p (g cc) -> p g cc", cc=128)[:, :, 0:64],
                     lo8.rearrange("p (g cc) -> p g cc", cc=64), AF.Copy)
        return e.activation(
            zs_d[:].rearrange("p (g cc) -> p g cc", cc=128)[:, :, 64:128],
            hi8.rearrange("p (g cc) -> p g cc", cc=64), AF.Copy)
    say("scalar", zdcv, waits=[("v", now("v"))], inc=("s", 1))

    def zdmul(e):
        return e.scalar_tensor_tensor(zs_d[:], zs_d[:], -1.0, sd_bf[:],
                                      OP.mult, OP.mult)
    say("vector", zdmul, waits=[("s", now("s")), ("sdl", now("sdl"))],
        inc=("v", 1))

    def hs0(e):
        return e.memset(hsum_f[:], 0.0)
    say("vector", hs0, inc=("v", 1))

    if stop_after == 80:
        return _finish(nc, S, st, stB)

    recb = {}
    iter_i = 0
    for nb in range(NB):
        if nb % 11 == 0:
            base = nb
            for mi, _ in enumerate(["g", "u"]):
                def zwl(e, mi=mi, base=base):
                    lo_ = base * 256
                    hi_ = min(I_FF, lo_ + 2816)
                    return e.dma_start(
                        out=zswin[0:32, mi * 2816: mi * 2816 + (hi_ - lo_)],
                        in_=zs_sp[mi * 32:(mi + 1) * 32, lo_:hi_])
                w = [("t", now("t"))]
                if base == 0:
                    w.append(("zpw", now("zpw")))
                say("sync", zwl, waits=w, inc=("zw", 16))
            zbase = base

        def qwl(e, nb=nb):
            return e.dma_start(out=qwbuf[:], in_=qwg_d[nb])
        say("sync", qwl, waits=[("t", recb.get(("mmdone", nb - 1, "g"),
                                              now("t")))], inc=("qg", 16))

        def qwl2b(e, nb=nb):
            return e.dma_start(out=qwdbuf[:, 0:KC * 32], in_=qwu_d[nb])
        say("sync", qwl2b, waits=[("t", recb.get(("mmdone", nb - 1, "u"),
                                                now("t")))], inc=("qu", 16))

        for mi, mat in enumerate(["g", "u"]):
            sdram = sgt_d if mat == "g" else sut_d

            def stl(e, nb=nb, mi=mi, sdram=sdram):
                last = None
                for rep in range(4):
                    last = e.dma_start(
                        out=stile[:, mi * 2048 + (nb % 2) * 1024 + rep * 256:
                                  mi * 2048 + (nb % 2) * 1024 + (rep + 1) * 256],
                        in_=sdram[:, nb * 256:(nb + 1) * 256])
                return last
            say("gpsimd", stl, waits=[("v", recb.get(("sm", nb - 2, mat), 0))],
                inc=("st%d" % (nb % 2), 64))

        for mi, mat in enumerate(["g", "u"]):
            qb = qwbuf if mat == "g" else qwdbuf[:, 0:KC * 32]
            ppg = 0 if mat == "g" else 0
            for blk in range(8):
                par = blk % 2

                def vunp(e, blk=blk, qb=qb, par=par):
                    src = qb[:, blk * 128:(blk + 1) * 128]
                    e.tensor_scalar(unp[:, par * 128:(par + 1) * 128],
                                    src, 0x0F0F0F0F, None, OP.bitwise_and)
                    return e.tensor_scalar(
                        scr[0:128, 3072 + par * 128:3072 + (par + 1) * 128
                            ].bitcast(I32)[:, 0:128],
                        src, 4, 0x0F0F0F0F,
                        OP.logical_shift_right, OP.bitwise_and)
                w = [("t", recb.get(("mm", iter_i - 2), 0)),
                     ("qg" if mat == "g" else "qu",
                      now("qg" if mat == "g" else "qu"))]
                say("vector", vunp, waits=w, inc=("v", 1))

                def scv(e, blk=blk, par=par):
                    lo8 = unp[:, par * 128:(par + 1) * 128].bitcast(I8)
                    hi8 = scr[0:128, 3072 + par * 128:3072 + (par + 1) * 128
                              ].bitcast(I8)
                    dst = wdq[:, par * T:(par + 1) * T].rearrange(
                        "p (kc g cc) -> p kc g cc", kc=4, g=2)
                    e.activation(dst[:, :, :, 0:64],
                                 lo8.rearrange("p (kc g cc) -> p kc g cc",
                                               kc=4, g=2), AF.Copy)
                    return e.activation(
                        dst[:, :, :, 64:128],
                        hi8.rearrange("p (kc g cc) -> p kc g cc", kc=4, g=2),
                        AF.Copy)
                say("scalar", scv, waits=[("v", now("v"))], inc=("s", 1))

                def smu(e, nb=nb, mi=mi, par=par):
                    return e.tensor_tensor(
                        wdq[:, par * T:(par + 1) * T],
                        wdq[:, par * T:(par + 1) * T],
                        stile[:, mi * 2048 + (nb % 2) * 1024:
                              mi * 2048 + (nb % 2) * 1024 + 1024], OP.mult)
                say("vector", smu,
                    waits=[("s", now("s")),
                           ("st%d" % (nb % 2), now("st%d" % (nb % 2)))],
                    inc=("v", 1))
                recb[("sm", nb, mat)] = now("v")

                def pmm(e, blk=blk, par=par):
                    last = None
                    for k4 in range(4):
                        kc = blk * 4 + k4
                        for sub in range(2):
                            lhs = wdq[:, par * T + k4 * 256 + sub * 128:
                                      par * T + k4 * 256 + (sub + 1) * 128]
                            for th in range(2):
                                last = e.matmul(
                                    pf[0:128, sub * 1024 + th * 512:
                                       sub * 1024 + (th + 1) * 512],
                                    lhs,
                                    big0[:, kc * T + th * 512:
                                         kc * T + (th + 1) * 512],
                                    start=(kc == 0), stop=False)
                    return last
                wmm = [("v", now("v"))]
                if nb == 0 and mat == "g" and blk == 0:
                    wmm.append(("xq", now("xq")))
                if mat == "u" and blk == 0:
                    wmm.append(("s", recb[("silu", nb)]))
                say("tensor", pmm, waits=wmm, inc=("t", 1))
                recb[("mm", iter_i)] = now("t")
                iter_i += 1

            def zmm(e, nb=nb, mi=mi, zbase=zbase):
                off = mi * 2816 + (nb - zbase) * 256
                last = None
                for sub in range(2):
                    for th in range(2):
                        last = e.matmul(
                            pf[0:128, sub * 1024 + th * 512:
                               sub * 1024 + (th + 1) * 512],
                            zswin[0:32, off + sub * 128: off + (sub + 1) * 128],
                            xsum_bf[0:32, th * 512:(th + 1) * 512],
                            start=False, stop=True)
                return last
            say("tensor", zmm, waits=[("zw", now("zw"))], inc=("t", 1))
            recb[("mmdone", nb, mat)] = now("t")

            if mat == "g":
                def silu(e):
                    e.activation(sgb[:, 0:T], pf[0:128, 0:1024], AF.Silu)
                    return e.activation(sgb[:, T:2 * T], pf[0:128, 1024:2048],
                                        AF.Silu)
                say("scalar", silu, waits=[("t", now("t"))], inc=("s", 1))
                recb[("silu", nb)] = now("s")
            else:
                def htt(e, nb=nb):
                    hs = (nb % 4) * T
                    e.tensor_tensor(hring[:, (nb % 2) * 2 * T:
                                          (nb % 2) * 2 * T + T],
                                    pf[0:128, 0:1024], sgb[:, 0:T], OP.mult)
                    return e.tensor_tensor(
                        hring[:, (nb % 2) * 2 * T + T:(nb % 2) * 2 * T + 2 * T],
                        pf[0:128, 1024:2048], sgb[:, T:2 * T], OP.mult)
                say("vector", htt, waits=[("t", now("t")),
                                          ("s", recb[("silu", nb)]),
                                          ("hw", recb.get(("hsp", nb - 2), 0))],
                    inc=("v", 1))

                def hsp(e, nb=nb):
                    hs = (nb % 2) * 2 * T
                    e.dma_start(out=h_sp[nb * 256:nb * 256 + 128, :],
                                in_=hring[:, hs:hs + T])
                    return e.dma_start(out=h_sp[nb * 256 + 128:(nb + 1) * 256, :],
                                       in_=hring[:, hs + T:hs + 2 * T])
                say("sync", hsp, waits=[("v", now("v"))], inc=("hw", 32))
                recb[("hsp", nb)] = now("hw")

                def hsmm(e, nb=nb):
                    hs = (nb % 2) * 2 * T
                    last = None
                    for sub in range(2):
                        g = nb * 2 + sub
                        for th in range(2):
                            last = e.matmul(
                                pf[0:86, 2048 + th * 512:2560 + th * 512],
                                c_oh[:, 85 - g + 0:171 - g],
                                hring[:, hs + sub * T + th * 512:
                                      hs + sub * T + (th + 1) * 512],
                                start=(sub == 0), stop=(sub == 1))
                    return last
                say("tensor", hsmm, waits=[("v", now("v"))], inc=("t", 1))

                def hacc(e):
                    return e.tensor_tensor(hsum_f[:], hsum_f[:],
                                           pf[0:86, 2048:3072], OP.add)
                say("vector", hacc, waits=[("t", now("t"))], inc=("v", 1))

    def hcast(e):
        return e.tensor_copy(hsum_bf[:], hsum_f[:])
    say("vector", hcast, inc=("v", 1))

    if dbg:
        def d9(e):
            last = None
            for g in range(GFF):
                last = e.dma_start(
                    out=dbg_outs["dbg_h"][g * 128:(g + 1) * 128, :],
                    in_=h_sp[g * 128:(g + 1) * 128, :])
            return last
        say("sync", d9, waits=[("dq", now("dq"))], inc=("dq", 16 * GFF))

    if dbg:
        def dhs(e):
            e.dma_start(out=dbg_outs["dbg_hs"][:], in_=hsum_bf[:])
            return e.dma_start(out=dbg_outs["dbg_zsd"][:], in_=zs_d[:])
        say("sync", dhs, waits=[("v", now("v"))], inc=("dq", 32))

    if stop_after == 81:
        return _finish(nc, S, st, stB)

    def stat20(e):
        e.memset(scr[0:1, 3072:4096], 0.0)
        return e.memset(scr[0:1, 4096:5120], 0.0)
    say("vector", stat20, inc=("v", 1))

    recd = {}
    gi = 0
    for ob in range(OBN):
        def qdl(e, ob=ob):
            return e.dma_start(out=qwdbuf[:], in_=qwd_d[ob])
        say("sync", qdl, waits=[("t", recd.get(("mm", (ob - 1, GFF - 1)),
                                              now("t")))], inc=("qdw", 16))

        def x1l2(e, ob=ob):
            e.dma_start(out=x1ring[:, 0:T],
                        in_=x1_sp[ob * 256:ob * 256 + 128, :])
            return e.dma_start(out=x1ring[:, T:2 * T],
                               in_=x1_sp[ob * 256 + 128:(ob + 1) * 256, :])
        w = [("v", recd.get(("ev", ob - 1), 0))]
        if ob == 0:
            w.append(("xw", now("xw")))
        say("sync", x1l2, waits=w, inc=("xd", 32))

        for g in range(GFF):
            par = g % 2
            if g % 22 == 0:
                g0, g1 = g, min(GFF, g + 22)

                def sdf(e, ob=ob, g0=g0, g1=g1):
                    return e.dma_start(
                        out=rowb[0:1, 0:(g1 - g0) * 256],
                        in_=sd_bf[g0:g1, ob * 256:(ob + 1) * 256])
                say("scalar", sdf, waits=[("t", now("t"))], inc=("ds", 16))
                ds_tgt = now("ds")

            def hld(e, g=g):
                return e.dma_start(out=hring[:, (g % 4) * T:(g % 4 + 1) * T],
                                   in_=h_sp[g * 128:(g + 1) * 128, :])
            whl = [("t", recd.get(("mm", (ob, g - 4)),
                                  recd.get(("mm", (ob - 1, GFF - 4 + g)), 0)))]
            if ob == 0 and g == 0:
                whl.append(("hw", now("hw")))
            say("sync", hld, waits=whl, inc=("hl%d" % (g % 4), 16))
            hl_tgt = now("hl%d" % (g % 4))

            def sbmm(e, g=g, par=par):
                base = 2048 if par == 0 else 2560
                return e.matmul(pf[0:128, base:base + 256], c_o1b[:],
                                rowb[0:1, (g % 22) * 256:(g % 22 + 1) * 256],
                                start=True, stop=True)
            wsb = [("ds", ds_tgt), ("v", recd.get(("sm", gi - 2), 0))]
            if g == 0:
                wsb = [("ds", ds_tgt), ("v", now("v"))]
            say("tensor", sbmm, waits=wsb, inc=("t", 1))
            t_sb = now("t")

            def vu(e, g=g, par=par):
                src = qwdbuf[:, g * 32:(g + 1) * 32]
                e.tensor_scalar(unpd[:, par * 32:(par + 1) * 32], src,
                                0x0F0F0F0F, None, OP.bitwise_and)
                return e.tensor_scalar(
                    scr[0:128, par * 32:(par + 1) * 32].bitcast(I32)[:, 0:32],
                    src, 4, 0x0F0F0F0F, OP.logical_shift_right, OP.bitwise_and)
            say("vector", vu, waits=[("qdw", now("qdw")),
                                     ("s", recd.get(("cv", gi - 2), 0))],
                inc=("v", 1))

            def dcv(e, par=par):
                lo8 = unpd[:, par * 32:(par + 1) * 32].bitcast(I8)
                hi8 = scr[0:128, par * 32:(par + 1) * 32].bitcast(I8)
                dst = wdq2[:, par * 256:(par + 1) * 256].rearrange(
                    "p (g cc) -> p g cc", g=2)
                e.activation(dst[:, :, 0:64],
                             lo8.rearrange("p (g cc) -> p g cc", g=2), AF.Copy)
                return e.activation(dst[:, :, 64:128],
                                    hi8.rearrange("p (g cc) -> p g cc", g=2),
                                    AF.Copy)
            wdcv = [("v", now("v")),
                    ("t", recd.get(("mm", (ob, g - 2)),
                                   recd.get(("mm", (ob - 1, GFF - 2 + g)), 0)))]
            if dbg:
                wdcv.append(("db", now("db")))
            say("scalar", dcv, waits=wdcv, inc=("s", 1))
            recd[("cv", gi)] = now("s")
            if dbg and ob == 0 and g < 8:
                def dnib(e, g=g, par=par):
                    return e.dma_start(out=dbg_outs["dbg_wd"][:, g * 256:(g + 1) * 256],
                                       in_=wdq2[:, par * 256:(par + 1) * 256])
                say("sync", dnib, waits=[("s", now("s"))], inc=("db", 16))

            def smu2(e, par=par):
                base = 2048 if par == 0 else 2560
                return e.tensor_tensor(wdq2[:, par * 256:(par + 1) * 256],
                                       wdq2[:, par * 256:(par + 1) * 256],
                                       pf[0:128, base:base + 256], OP.mult)
            if NO_SB:
                def smu2(e, par=par):
                    return e.nop()
            say("vector", smu2, waits=[("s", now("s")), ("t", t_sb),
                                       ("db", now("db"))],
                inc=("v", 1))
            recd[("sm", gi)] = now("v")
            if dbg and ob == 0 and g < 8:
                def dsc(e, g=g, par=par):
                    return e.dma_start(out=dbg_outs["dbg_wd"][:, 2048 + g * 256:2048 + (g + 1) * 256],
                                       in_=wdq2[:, par * 256:(par + 1) * 256])
                say("sync", dsc, waits=[("v", now("v"))], inc=("db", 16))

            def dmm2(e, g=g, par=par):
                last = None
                for sub in range(2):
                    lhs = wdq2[:, par * 256 + sub * 128:
                               par * 256 + (sub + 1) * 128]
                    for th in range(2):
                        last = e.matmul(
                            pf[0:128, sub * 1024 + th * 512:
                               sub * 1024 + (th + 1) * 512],
                            lhs,
                            hring[:, (g % 4) * T + th * 512:
                                  (g % 4) * T + (th + 1) * 512],
                            start=(g == 0),
                            stop=bool(NO_ZS and g == GFF - 1))
                return last
            say("tensor", dmm2,
                waits=[("v", now("v")), ("hl%d" % (g % 4), hl_tgt)],
                inc=("t", 1))
            recd[("mm", (ob, g))] = now("t")
            gi += 1

        def zdm(e, ob=ob):
            if NO_ZS:
                return e.nop()
            last = None
            for sub in range(2):
                for th in range(2):
                    last = e.matmul(
                        pf[0:128, sub * 1024 + th * 512:
                           sub * 1024 + (th + 1) * 512],
                        zs_d[:, ob * 256 + sub * 128: ob * 256 + (sub + 1) * 128],
                        hsum_bf[0:86, th * 512:(th + 1) * 512],
                        start=False, stop=True)
            return last
        say("tensor", zdm, waits=[("v", now("v"))], inc=("t", 1))
        recd[("mm", (ob, GFF - 1))] = now("t")

        def evr(e, ob=ob):
            for sub in range(2):
                ch = ob * 2 + sub
                e.tensor_tensor(big0[:, ch * T:(ch + 1) * T],
                                pf[0:128, sub * 1024:(sub + 1) * 1024],
                                x1ring[:, sub * T:(sub + 1) * T], OP.add)
            e.tensor_tensor(scrb[:, 0:T], big0[:, (ob * 2) * T:(ob * 2 + 1) * T],
                            big0[:, (ob * 2) * T:(ob * 2 + 1) * T], OP.mult)
            return e.tensor_tensor(scrb[:, T:2 * T],
                                   big0[:, (ob * 2 + 1) * T:(ob * 2 + 2) * T],
                                   big0[:, (ob * 2 + 1) * T:(ob * 2 + 2) * T],
                                   OP.mult)
        say("vector", evr, waits=[("t", now("t")), ("xd", now("xd"))],
            inc=("v", 1))
        recd[("ev", ob)] = now("v")

        for sub in range(2):
            for th in range(2):
                def s2mm(e, ob=ob, sub=sub, th=th):
                    ch = ob * 2 + sub
                    e.matmul(pf[0:1, 2048:2560], c_ocb[:],
                             big0[:, ch * T + th * 512: ch * T + (th + 1) * 512],
                             start=True, stop=True)
                    return e.matmul(pf[0:1, 2560:3072], c_ocb[:],
                                    scrb[:, sub * T + th * 512:
                                         sub * T + (th + 1) * 512],
                                    start=True, stop=True)
                say("tensor", s2mm, waits=[("v", now("v"))], inc=("t", 1))

                def s2ac(e, th=th):
                    e.tensor_tensor(
                        scr[0:1, 3072 + th * 512:3072 + (th + 1) * 512],
                        scr[0:1, 3072 + th * 512:3072 + (th + 1) * 512],
                        pf[0:1, 2048:2560], OP.add)
                    return e.tensor_tensor(
                        scr[0:1, 4096 + th * 512:4096 + (th + 1) * 512],
                        scr[0:1, 4096 + th * 512:4096 + (th + 1) * 512],
                        pf[0:1, 2560:3072], OP.add)
                say("vector", s2ac, waits=[("t", now("t"))], inc=("v", 1))

    if stop_after == 82:
        return _finish(nc, S, st, stB)

    if dbg:
        def dst2(e):
            e.dma_start(out=dbg_outs["dbg_st2"][0:1, 0:1024],
                        in_=scr[0:1, 3072:4096])
            e.dma_start(out=dbg_outs["dbg_st2"][0:1, 1024:2048],
                        in_=scr[0:1, 4096:5120])
            last = None
            for ch in range(KC):
                last = e.dma_start(
                    out=dbg_outs["dbg_pre2"][ch * 128:(ch + 1) * 128, :],
                    in_=big0[:, ch * T:(ch + 1) * T])
            return last
        say("sync", dst2, waits=[("v", now("v")), ("t", now("t"))],
            inc=("dq", 16 * (KC + 2)))

    def ln2fin(e):
        mu = scr[0:1, 3072:4096]
        e.tensor_scalar(mu, mu, 1.0 / HID, None, OP.mult)
        ex2 = scr[0:1, 4096:5120]
        e.tensor_scalar(ex2, ex2, 1.0 / HID, None, OP.mult)
        mu2 = scr[0:1, 2048:3072]
        e.tensor_tensor(mu2, mu, mu, OP.mult)
        return e.tensor_tensor(ex2, ex2, mu2, OP.subtract)
    say("vector", ln2fin, inc=("v", 1))

    def ln2rs(e):
        v_ = scr[0:1, 4096:5120]
        e.activation(v_, v_, AF.Ln, bias=LN_EPS)
        return e.activation(v_, v_, AF.Exp, scale=-0.5)
    say("scalar", ln2rs, waits=[("v", now("v"))], inc=("s", 1))

    def nmr2(e):
        nm = scr[0:1, 3072:4096]
        e.tensor_tensor(nm, nm, scr[0:1, 4096:5120], OP.mult)
        return e.tensor_scalar(nm, nm, -1.0, None, OP.mult)
    say("vector", nmr2, waits=[("s", now("s"))], inc=("v", 1))

    def bcmm2(e):
        e.matmul(pf[0:128, 2048:2560], c_o1f[:], scr[0:1, 4096:4608],
                 start=True, stop=True)
        e.matmul(pf[0:128, 2560:3072], c_o1f[:], scr[0:1, 4608:5120],
                 start=True, stop=True)
        e.matmul(pf[0:128, 0:512], c_o1f[:], scr[0:1, 3072:3584],
                 start=True, stop=True)
        return e.matmul(pf[0:128, 512:1024], c_o1f[:], scr[0:1, 3584:4096],
                        start=True, stop=True)
    say("tensor", bcmm2, waits=[("v", now("v"))], inc=("t", 1))

    def bcev2(e):
        e.tensor_copy(scr[0:128, 0:T], pf[0:128, 2048:3072])
        return e.tensor_copy(scr[0:128, T:2 * T], pf[0:128, 0:1024])
    say("vector", bcev2, waits=[("t", now("t"))], inc=("v", 1))

    for ch in range(KC):
        def oap(e, ch=ch):
            t1 = scr[0:128, 2 * T + (ch % 2) * T:2 * T + (ch % 2 + 1) * T]
            e.tensor_tensor(t1, big0[:, ch * T:(ch + 1) * T],
                            scr[0:128, 0:T], OP.mult)
            return e.tensor_tensor(t1, t1, scr[0:128, T:2 * T], OP.add)
        w = [("od", recd.get(("od", ch - 2), 0))] if ch >= 2 else []
        say("vector", oap, waits=w, inc=("v", 1))

        def odma(e, ch=ch):
            return e.dma_start(
                out=out_d[ch * 128:(ch + 1) * 128, :],
                in_=scr[0:128, 2 * T + (ch % 2) * T:2 * T + (ch % 2 + 1) * T])
        say("sync", odma, waits=[("v", now("v"))], inc=("od", 16))
        recd[("od", ch)] = now("od")

    stB.close()
    return _finish(nc, S, st, None)


def _finish(nc, S, st, stA):
    def fin(e):
        return e.nop()
    allsems = [(k, S.cnt[k]) for k in list(S.cnt.keys())
               if S.cnt[k] > 0 and k != "cc"]
    S.do("sync", fin, waits=allsems)
    if stA is not None:
        stA.close()
    with nc.Block() as block:
        S.emit(block)
    st.close()
    return nc


def kernel(**inputs):
    maps, eo = _host_prep(inputs)
    nc = build_nc(stop_after=STOP_AFTER, dbg=1)
    res = run_bass_kernel_spmd(nc, maps, core_ids=list(range(NCORE)))
    out = np.empty((B_, L_, HID), dtype=np.float32)
    for r in range(NCORE):
        b, q = r // 4, r % 4
        dev = np.asarray(res.results[r]["out"])
        blk = np.empty((T, HID), dtype=np.float32)
        blk[:, eo] = dev.T
        out[b, q * T:(q + 1) * T, :] = blk
    return out
